# version 05 (final)
# baseline (speedup 1.0000x reference)
import numpy as np
import ml_dtypes
import concourse.bass as bass
import concourse.mybir as mybir
from concourse.bass_utils import run_bass_kernel_spmd
from contextlib import ExitStack

F32 = mybir.dt.float32
BF16 = mybir.dt.bfloat16
I32 = mybir.dt.int32
ALU = mybir.AluOpType
AF = mybir.ActivationFunctionType
AX = mybir.AxisListType

D = 2048
L = 4096
TOK = 2048
CHG = 8
GU = 2048 * 3072
DN = 1536 * 2048
SHGU_SL = 2048 * 384
SHDN_SL = 1536 * 256
RBLK1 = 2 * GU
RBLK2 = 2 * DN + SHGU_SL + SHDN_SL
OFF_DN = 0
OFF_SGU = 2 * DN
OFF_SDN = OFF_SGU + SHGU_SL
P126 = float(2.0 ** 126)
NE = 17

CONST_SHAPES = [
    ("DFTc", [128, 256]), ("T_reB", [128, 512]), ("T_imB", [128, 512]),
    ("Ti_reB", [128, 512]), ("Ti_imB", [128, 512]),
    ("B_re", [128, 128]), ("B_im", [128, 128]),
    ("C_re", [128, 128]), ("C_im", [128, 128]),
    ("ID_re", [128, 128]), ("ID_imN", [128, 128]),
    ("I128", [128, 128]), ("I64", [64, 64]), ("ONESR", [1, 128]),
]


def _consts():
    n2 = np.arange(128)
    k2 = np.arange(128)
    th = 2 * np.pi * np.outer(n2, k2) / 128.0
    DFTc = np.concatenate([np.cos(th), -np.sin(th)], axis=1)
    p = np.arange(128) % 64
    thT = 2 * np.pi * np.outer(p, k2) / 8192.0
    zpad = np.zeros((128, 128))

    def twb(t):
        return np.tile(np.concatenate([t, zpad], axis=1), (1, 2))

    T_reB = twb(np.cos(thT))
    T_imB = twb(-np.sin(thT))
    Ti_reB = twb(np.cos(thT))
    Ti_imB = twb(np.sin(thT))
    q = np.arange(128) % 64
    half = (np.arange(128)[:, None] // 64) == (np.arange(128)[None, :] // 64)
    thB = 2 * np.pi * np.outer(q, q) / 64.0
    B_re = np.cos(thB) * half
    B_im = -np.sin(thB) * half
    C_re = np.cos(thB) * half / 64.0
    C_im = np.sin(thB) * half / 64.0
    thI = 2 * np.pi * np.outer(k2, n2) / 128.0
    ID_re = np.cos(thI) / 128.0
    ID_imN = -np.sin(thI) / 128.0
    out = dict(DFTc=DFTc, T_reB=T_reB, T_imB=T_imB, Ti_reB=Ti_reB, Ti_imB=Ti_imB,
               B_re=B_re, B_im=B_im, C_re=C_re, C_im=C_im,
               ID_re=ID_re, ID_imN=ID_imN, I128=np.eye(128), I64=np.eye(64),
               ONESR=np.ones((1, 128)))
    return {k: np.ascontiguousarray(v, np.float32) for k, v in out.items()}


PHASE_LIMIT = 4
DEQUANT_ON = True
AG_ON = True
TOUCH_ALL = False
KF = set()
DEBUG_TAPS = False


def _build_nc():
    nc = bass.Bass("TRN2", target_bir_lowering=False, num_devices=8, detect_race_conditions=False)

    xb_d = nc.dram_tensor("xb", [L, D], F32, kind="ExternalInput")
    xo_d = nc.dram_tensor("xo", [TOK, D], F32, kind="ExternalInput")
    wT_d = {s: nc.dram_tensor("w" + s + "T", [D, 1024], F32, kind="ExternalInput")
            for s in "qkv"}
    bias_d = {s: nc.dram_tensor("b" + s, [1024, 1], F32, kind="ExternalInput")
              for s in "qkv"}
    cw_d = {s: nc.dram_tensor("c" + s + "w", [1024, 3], F32, kind="ExternalInput")
            for s in "qk"}
    cb_d = {s: nc.dram_tensor("c" + s + "b", [1024, 1], F32, kind="ExternalInput")
            for s in "qk"}
    sker_d = nc.dram_tensor("sker", [1024, 2048], F32, kind="ExternalInput")
    dsk_d = nc.dram_tensor("dsk", [1024, 1], F32, kind="ExternalInput")
    woT_d = nc.dram_tensor("woT", [1024, D], F32, kind="ExternalInput")
    bo_d = nc.dram_tensor("bo", [1, D], F32, kind="ExternalInput")
    bomask_d = nc.dram_tensor("bomask", [1, 1], F32, kind="ExternalInput")
    gwT_d = nc.dram_tensor("gwT", [D, 16], F32, kind="ExternalInput")
    colv_d = nc.dram_tensor("colvf", [128, 16], F32, kind="ExternalInput")
    gupk_d = nc.dram_tensor("gupk", [2, 256, 3072], I32, kind="ExternalInput")
    gusc_d = nc.dram_tensor("gusc", [2, 16, 3072], F32, kind="ExternalInput")
    dnpk_d = nc.dram_tensor("dnpk", [2, 192, 2048], I32, kind="ExternalInput")
    dnsc_d = nc.dram_tensor("dnsc", [2, 12, 2048], F32, kind="ExternalInput")
    sgupk_d = nc.dram_tensor("sgupk", [256, 384], I32, kind="ExternalInput")
    sgusc_d = nc.dram_tensor("sgusc", [16, 384], F32, kind="ExternalInput")
    sdnpk_d = nc.dram_tensor("sdnpk", [192, 256], I32, kind="ExternalInput")
    sdnsc_d = nc.dram_tensor("sdnsc", [12, 256], F32, kind="ExternalInput")
    cst_d = {n: nc.dram_tensor(n, s, F32, kind="ExternalInput") for n, s in CONST_SHAPES}
    out_d = nc.dram_tensor("out", [TOK, D], F32, kind="ExternalOutput")
    dbg_d = {}
    if DEBUG_TAPS:
        dbg_d["dbg_xnTa"] = nc.dram_tensor("dbg_xnTa", [D, L], BF16, kind="ExternalOutput")
        dbg_d["dbg_xnTo"] = nc.dram_tensor("dbg_xnTo", [D, TOK], BF16, kind="ExternalOutput")
        dbg_d["dbg_spart"] = nc.dram_tensor("dbg_spart", [L, D], BF16, kind="ExternalOutput")
        dbg_d["dbg_sred"] = nc.dram_tensor("dbg_sred", [1, TOK * D], BF16, kind="ExternalOutput")
        dbg_d["dbg_cc"] = nc.dram_tensor("dbg_cc", [1, 2 ** 20], BF16, kind="ExternalOutput")
        dbg_d["dbg_led"] = nc.dram_tensor("dbg_led", [128, 256], BF16, kind="ExternalOutput")

    xnTa_d = nc.dram_tensor("xnTa", [D, L], BF16)
    xnTo_d = nc.dram_tensor("xnTo", [D, TOK], BF16)
    spart_d = nc.dram_tensor("spart", [L, D], BF16)
    sred_d = nc.dram_tensor("sred", [1, TOK * D], BF16)
    ccin1_d = nc.dram_tensor("ccin1", [1, RBLK1], BF16)
    ccout1_d = nc.dram_tensor("ccout1", [8, RBLK1], BF16)
    ccin2_d = nc.dram_tensor("ccin2", [1, RBLK2], BF16)
    ccout2_d = nc.dram_tensor("ccout2", [8, RBLK2], BF16)

    ctx = ExitStack()
    e = ctx.enter_context

    Aa = e(nc.sbuf_tensor("Aa", [128, 16384], BF16))
    Bb = e(nc.sbuf_tensor("Bb", [128, 24576], BF16))
    Cc = e(nc.sbuf_tensor("Cc", [128, 8192], BF16))
    Dd = e(nc.sbuf_tensor("Dd", [128, 8192], BF16))
    Ff = e(nc.sbuf_tensor("Ff", [128, 20480], BF16))
    Gg = e(nc.sbuf_tensor("Gg", [128, 4096], F32))
    Hh = e(nc.sbuf_tensor("Hh", [128, 4096], BF16))
    SBs = e(nc.sbuf_tensor("SBs", [128, 4608], BF16))
    stg = SBs[:, 0:4096]
    ebuf = SBs[:, 0:1024]
    e2buf = SBs[:, 1024:3072]
    tmpA = SBs[:, 3072:3584]
    tmpU = SBs[:, 3584:4096]
    tmpV = SBs[:, 4096:4608]
    xin = Aa[:].bitcast(F32)[:, 0:4096]
    xfin = Ff[:].bitcast(F32)[:, 8192:10240]
    accA = Ff[:].bitcast(F32)[:, 0:8192]
    xnb = Hh[:, 0:4096]
    comb = e(nc.sbuf_tensor("comb", [128, 64], F32))
    lgb = e(nc.sbuf_tensor("lgb", [128, 128], F32))
    redb = e(nc.sbuf_tensor("redb", [128, 16], F32))
    sW = Gg[:].bitcast(BF16)
    skerb = e(nc.sbuf_tensor("skerb", [128, 2048], BF16))
    svec = e(nc.sbuf_tensor("svec", [128, 32], F32))
    sbo = e(nc.sbuf_tensor("sbo", [1, 2048], BF16))
    sboraw = e(nc.sbuf_tensor("sboraw", [1, 2048], BF16))
    sgwT = e(nc.sbuf_tensor("sgwT", [128, 256], BF16))
    colv = e(nc.sbuf_tensor("colv", [128, 16], F32))
    scst = {n: e(nc.sbuf_tensor("s" + n, s, BF16)) for n, s in CONST_SHAPES}
    pb = [e(nc.psum_tensor(f"pb{i}", [128, 512], F32)) for i in range(8)]

    sem_names = [
        "ld", "dqM", "dqL", "dqV", "dqS0", "dqS1", "agC", "rsC",
        "lnL0", "lnL1", "lnV", "lnS", "lnT", "lnTe", "lnD0", "lnD1",
        "awL", "axL0", "axL1", "pjT", "pjS", "cvV",
        "preP", "preC", "kpreP", "kpreC", "s1P", "s1S", "s1V",
        "ks1P", "ks1S", "ks1V", "k2P", "k2S", "k2V",
        "s2P", "s2S", "s2V", "i1P", "i1S", "i1V",
        "mtP", "mtS", "i2P", "i2S", "ptP", "ptC", "yV",
        "woT_", "woS", "woE", "woD0", "woD1",
        "mxL", "mgwL0", "mgwL1", "mdwL", "mguT", "mguS", "mguV", "mdT", "mdV",
        "gT", "gV", "gS", "gSi",
        "fiL", "fiV", "outD",
    ]
    sems = {n: e(nc.semaphore(n)) for n in sem_names}
    Y2b = Cc[:, 0:8192]

    NCONST = len(CONST_SHAPES)
    Ai = Aa[:].bitcast(I32)
    Bi = Bb[:].bitcast(I32)
    Fi = Ff[:].bitcast(I32)

    DQJ = []
    for k in range(2):
        DQJ.append(dict(pk=gupk_d, pk_off=k * 256 * 3072, cols=3072,
                        sc=gusc_d, sc_off=k * 16 * 3072, sc_rows=16,
                        dstT=None, dst=k * GU, parts=128))
    for k in range(2):
        DQJ.append(dict(pk=dnpk_d, pk_off=k * 192 * 2048, cols=2048,
                        sc=dnsc_d, sc_off=k * 12 * 2048, sc_rows=12,
                        dstT=2, dst=OFF_DN + k * DN, parts=96))
    DQJ.append(dict(pk=sgupk_d, pk_off=0, cols=384, sc=sgusc_d, sc_off=0,
                    sc_rows=16, dstT=2, dst=OFF_SGU, parts=128))
    DQJ.append(dict(pk=sdnpk_d, pk_off=0, cols=256, sc=sdnsc_d, sc_off=0,
                    sc_rows=12, dstT=2, dst=OFF_SDN, parts=96))
    for j in DQJ:
        j["dstD"] = ccin2_d if j["dstT"] == 2 else ccin1_d


    def _emit_taps():
        if not DEBUG_TAPS:
            return
        blk4 = nc.Block("ph4")
        with blk4:
            @blk4.gpsimd
            def _(g):
                if PHASE_LIMIT >= 3:
                    g.wait_ge(sems["outD"], 16 * 16)
                elif PHASE_LIMIT >= 1:
                    g.wait_ge(sems["lnD0"], 16 * 24)
                    g.wait_ge(sems["lnD1"], 16 * 24)
                elif DEQUANT_ON:
                    g.wait_ge(sems["dqS0"], 16 * 4 * len(DQJ))
                    g.wait_ge(sems["dqS1"], 16 * 4 * len(DQJ))
                g.dma_start(dbg_d["dbg_xnTa"].ap(), xnTa_d.ap()).then_inc(sems["ld"], 16)
                g.dma_start(dbg_d["dbg_xnTo"].ap(), xnTo_d.ap()).then_inc(sems["ld"], 16)
                g.dma_start(dbg_d["dbg_spart"].ap(), spart_d.ap()).then_inc(sems["ld"], 16)
                g.dma_start(dbg_d["dbg_sred"].ap(), sred_d.ap()).then_inc(sems["ld"], 16)
                g.dma_start(dbg_d["dbg_cc"].ap(), ccout1_d.ap().rearrange("a b -> (a b)", )[0:2 ** 20].rearrange("(a b) -> a b", a=1)).then_inc(sems["ld"], 16)
                g.dma_start(dbg_d["dbg_led"].ap(), Y2b[:, 0:256]).then_inc(sems["ld"], 16)
                g.wait_ge(sems["ld"], 16 * (len(CONST_SHAPES) + 2) + 16 * 6)

    blkI = nc.Block("init")
    with blkI:
        @blkI.scalar
        def _(s):
            for fn in (AF.Exp, AF.Silu, AF.Copy, AF.Square, AF.Sqrt):
                s.activation(lgb[:, 16:32], lgb[:, 0:16], fn, scale=0.0, bias=1.0)
            s.drain()

    blk0 = nc.Block("ph0")
    with blk0:
        @blk0.gpsimd
        def _(g):
            if "K1" not in KF:
                for n in scst:
                    g.dma_start(scst[n][:], cst_d[n][:]).then_inc(sems["ld"], 16)
            if "K2" not in KF:
                gsrc = bass.AP(gwT_d, 0, [[16, 128], [2048, 16], [1, 16]])
                g.dma_start(sgwT[:], gsrc).then_inc(sems["ld"], 16)
                g.dma_start(sboraw[0:1, 0:2048], bo_d[:]).then_inc(sems["ld"], 16)

        if TOUCH_ALL:
            @blk0.scalar
            def _(s2):
                alltens = ([xb_d, xo_d, sker_d, woT_d, bo_d, bomask_d, gwT_d, colv_d,
                            gusc_d, dnsc_d, sgusc_d, sdnsc_d]
                           + list(wT_d.values()) + list(bias_d.values())
                           + list(cw_d.values()) + list(cb_d.values()) + [dsk_d]
                           + list(cst_d.values()))
                allint = [gupk_d, dnpk_d, sgupk_d, sdnpk_d]
                n = 0
                for td in alltens:
                    s2.dma_start(e2buf[0:1, 0:2].bitcast(F32), bass.AP(td, 0, [[1, 1], [1, 1]])).then_inc(sems["fiL"], 16)
                    n += 1
                for td in allint:
                    s2.dma_start(e2buf[0:1, 0:2].bitcast(I32), bass.AP(td, 0, [[1, 1], [1, 1]])).then_inc(sems["fiL"], 16)
                    n += 1
                if PHASE_LIMIT < 3:
                    s2.dma_start(bass.AP(out_d, 0, [[1, 1], [1, 1]]),
                                 e2buf[0:1, 0:2].bitcast(F32)).then_inc(sems["fiL"], 16)
                    n += 1
                s2.wait_ge(sems["fiL"], 16 * n)

        @blk0.sync
        def _(sy):
            if "K3" in KF:
                return
            sy.dma_start(colv[:], colv_d[:]).then_inc(sems["dqM"], 16)
            sy.dma_start(svec[0:1, 15:16], bomask_d[:]).then_inc(sems["dqM"], 16)
            nst = 0
            for ji, j in enumerate(DQJ if DEQUANT_ON else []):
                p_, cols = j["parts"], j["cols"]
                w2 = 2 * cols
                if ji > 0:
                    sy.wait_ge(sems["dqS0"], 16 * 4 * ji)
                    sy.wait_ge(sems["dqS1"], 16 * 4 * ji)
                src = bass.AP(j["pk"], j["pk_off"], [[w2, p_], [1, w2]])
                sy.dma_start(Ai[0:p_, 0:w2], src).then_inc(sems["dqL"], 16)
                ssrc = bass.AP(j["sc"], j["sc_off"],
                               [[cols, j["sc_rows"]], [0, p_ // j["sc_rows"]], [1, cols]])
                sy.dma_start(Gg[0:p_, 0:cols], ssrc).then_inc(sems["dqL"], 16)
                for jn in range(8):
                    nst += 1
                    sy.wait_ge(sems["dqV"], nst)
                    dqbuf = Cc if jn % 2 == 0 else Dd
                    dst = bass.AP(j["dstD"], j["dst"] + jn * cols,
                                  [[16 * cols, p_], [8 * cols, 2], [1, cols]])
                    sy.dma_start(dst, dqbuf[0:p_, 0:w2]).then_inc(
                        sems["dqS0" if jn % 2 == 0 else "dqS1"], 16)
            if DEQUANT_ON:
                sy.wait_ge(sems["dqS0"], 16 * 4 * len(DQJ))
                sy.wait_ge(sems["dqS1"], 16 * 4 * len(DQJ))

        @blk0.vector
        def _(v):
            if "K4" in KF:
                return
            v.memset(svec[:, 14:15], 1e-5)
            v.memset(stg, 2.0)
            if "K1" not in KF and "K2" not in KF:
                v.wait_ge(sems["ld"], 16 * (len(CONST_SHAPES) + 2))
            if "K3" not in KF:
                v.wait_ge(sems["dqM"], 32)
            v.tensor_scalar(sbo[0:1, :], sboraw[0:1, 0:2048], svec[0:1, 15:16],
                            None, ALU.mult)
            nib = 0
            for ji, j in enumerate(DQJ if DEQUANT_ON else []):
                p_, cols = j["parts"], j["cols"]
                w2 = 2 * cols
                v.wait_ge(sems["dqL"], 16 * (2 * ji + 2))
                spk = Ai[0:p_, 0:w2]
                t1 = Bi[0:p_, 0:w2]
                t2 = Fi[0:p_, 0:w2]
                ssc = Gg[0:p_, 0:cols]
                t3b = Bi[0:p_, 0:w2].bitcast(BF16)
                for jn in range(8):
                    if nib >= 2:
                        v.wait_ge(sems["dqS0" if nib % 2 == 0 else "dqS1"],
                                  16 * ((nib - 2) // 2 + 1))
                    dqv = (Cc if jn % 2 == 0 else Dd)[0:p_, 0:w2]
                    sa = 6 - 4 * jn
                    if sa >= 0:
                        v.tensor_scalar(t1, spk, sa, 0x1C0, ALU.logical_shift_left, ALU.bitwise_and)
                    else:
                        v.tensor_scalar(t1, spk, -sa, 0x1C0, ALU.logical_shift_right, ALU.bitwise_and)
                    sb = 12 - 4 * jn
                    if sb >= 0:
                        v.tensor_scalar(t2, spk, sb, 0x8000, ALU.logical_shift_left, ALU.bitwise_and)
                    else:
                        v.tensor_scalar(t2, spk, -sb, 0x8000, ALU.logical_shift_right, ALU.bitwise_and)
                    v.tensor_tensor(t1, t1, t2, ALU.bitwise_or)
                    for w_ in range(2):
                        lo = t3b[:, w_ * w2: w_ * w2 + w2: 2]
                        cp = v.scalar_tensor_tensor(
                            dqv[:, w_ * cols:(w_ + 1) * cols], lo, P126, ssc,
                            ALU.mult, ALU.mult)
                    cp.then_inc(sems["dqV"], 1)
                    nib += 1

        @blk0.gpsimd
        def _(g):
            if not AG_ON:
                return
            g.wait_ge(sems["dqS0"], 16 * 4 * len(DQJ))
            g.wait_ge(sems["dqS1"], 16 * 4 * len(DQJ))
            g.collective_compute(
                "AllGather", ALU.bypass,
                replica_groups=[list(range(8))],
                ins=[ccin1_d.ap().opt()],
                outs=[ccout1_d.ap().opt()],
            ).then_inc(sems["agC"])
            g.collective_compute(
                "AllGather", ALU.bypass,
                replica_groups=[list(range(8))],
                ins=[ccin2_d.ap().opt()],
                outs=[ccout2_d.ap().opt()],
            ).then_inc(sems["agC"])

    if PHASE_LIMIT < 1:
        _emit_taps()
        ctx.close()
        return nc
    LNT = [("b", i) for i in range(32)] + [("o", i) for i in range(16)]
    blk1 = nc.Block("ph1")
    with blk1:
        @blk1.sync
        def _(sy):
            def store(ti):
                seg2, i2 = LNT[ti]
                sl = ti % 2
                sy.wait_ge(sems["lnTe"], ti + 1)
                if seg2 == "b":
                    dst = bass.AP(xnTa_d, i2 * 128, [[L, 128], [L * 128, 16], [1, 128]])
                else:
                    dst = bass.AP(xnTo_d, i2 * 128, [[TOK, 128], [TOK * 128, 16], [1, 128]])
                sy.dma_start(dst, stg[:, sl * 2048:(sl + 1) * 2048]).then_inc(
                    sems["lnD0" if ti % 2 == 0 else "lnD1"], 16)

            sy.dma_start(spart_d[0:128, :], stg[:, 0:2048]).then_inc(sems["lnL0"], 16)
            sy.wait_ge(sems["lnL0"], 16)
            for ti, (seg, i) in enumerate(LNT):
                src = xb_d if seg == "b" else xo_d
                if ti >= 2:
                    sy.wait_ge(sems["lnV"], 3 * (ti - 1))
                sy.dma_start(xin[:, (ti % 2) * 2048:(ti % 2 + 1) * 2048],
                             src[i * 128:(i + 1) * 128, :]).then_inc(
                    sems["lnL0" if ti % 2 == 0 else "lnL1"], 16)
                if ti >= 1:
                    store(ti - 1)
            store(len(LNT) - 1)
            sy.wait_ge(sems["lnD0"], 16 * (len(LNT) // 2))
            sy.wait_ge(sems["lnD1"], 16 * (len(LNT) // 2))

        @blk1.vector
        def _(v):
            def VJ():
                v.tensor_tensor(redb[:, 15:16], redb[:, 15:16], redb[:, 15:16], ALU.max)

            for ti, (seg, i) in enumerate(LNT):
                sl = ti % 2
                xt = xin[:, sl * 2048:(sl + 1) * 2048]
                xc = Gg[:, 0:2048]
                v.wait_ge(sems["lnL0" if ti % 2 == 0 else "lnL1"], 16 * (ti // 2 + 1) + (16 if ti % 2 == 0 else 0))
                v.reduce_sum(redb[:, 0:1], xt, axis=AX.X)
                VJ()
                v.tensor_scalar(redb[:, 1:2], redb[:, 0:1], 1.0 / 2048.0, None, ALU.mult)
                VJ()
                v.tensor_scalar(xc, xt, redb[:, 1:2], None, ALU.subtract).then_inc(sems["lnV"], 1)
                v.tensor_tensor(Gg[:, 2048:4096], xc, xc, ALU.mult)
                v.reduce_sum(redb[:, 2:3], Gg[:, 2048:4096], axis=AX.X).then_inc(sems["lnV"], 1)
                v.wait_ge(sems["lnS"], 2 * ti + 2)
                v.reciprocal(redb[:, 4:5], redb[:, 3:4])
                VJ()
                if ti >= 2:
                    v.wait_ge(sems["lnT"], 4 * (ti - 1))
                v.tensor_scalar(xnb[:, sl * 2048:(sl + 1) * 2048], xc, redb[:, 4:5],
                                None, ALU.mult).then_inc(sems["lnV"], 1)
                if DEBUG_TAPS and ti < 16:
                    v.tensor_copy(Y2b[:, ti * 8:ti * 8 + 5], redb[:, 0:5])
                if DEBUG_TAPS and ti == 0:
                    v.tensor_copy(Y2b[:, 128:256], xnb[:, 0:128])

        @blk1.scalar
        def _(s):
            for _w in range(4):
                s.activation(lgb[:, 16:32], lgb[:, 0:16], AF.Sqrt,
                             bias=svec[:, 14:15], scale=1.0 / 2048.0)
            s.drain()

            def evac(ti):
                sl = ti % 2
                s.wait_ge(sems["lnT"], 4 * (ti + 1))
                if ti >= 2:
                    s.wait_ge(sems["lnD0" if ti % 2 == 0 else "lnD1"],
                              16 * ((ti - 2) // 2 + 1))
                bb = (ti % 2) * 4
                for bk in range(4):
                    s.copy(stg[:, sl * 2048 + bk * 512: sl * 2048 + (bk + 1) * 512],
                           pb[bb + bk][:, :])
                s.drain().then_inc(sems["lnTe"], 1)

            for ti, (seg, i) in enumerate(LNT):
                s.wait_ge(sems["lnV"], 3 * ti + 1)
                s.drain().then_inc(sems["lnS"], 1)
                s.wait_ge(sems["lnV"], 3 * ti + 2)
                s.activation(redb[:, 3:4], redb[:, 2:3], AF.Sqrt,
                             bias=svec[:, 14:15], scale=1.0 / 2048.0)
                s.drain().then_inc(sems["lnS"], 1)
                if ti >= 1:
                    evac(ti - 1)
            evac(len(LNT) - 1)

        @blk1.tensor
        def _(t):
            t.wait_ge(sems["ld"], 16 * (NCONST + 2))
            for ti, (seg, i) in enumerate(LNT):
                sl = ti % 2
                t.wait_ge(sems["lnV"], 3 * ti + 3)
                if ti >= 2:
                    t.wait_ge(sems["lnTe"], ti - 1)
                bb = (ti % 2) * 4
                for ht in range(16):
                    mm = t.matmul(pb[bb + ht // 4][:, (ht % 4) * 128:(ht % 4 + 1) * 128],
                                  xnb[:, sl * 2048 + ht * 128: sl * 2048 + (ht + 1) * 128],
                                  scst["I128"][:], start=True, stop=True)
                    if ht % 4 == 3:
                        mm.then_inc(sems["lnT"], 1)


    if PHASE_LIMIT < 2:
        _emit_taps()
        ctx.close()
        return nc
    PK, PV, PQ, PKC, PQC = 0, 1, 2, 3, 4

    def fsl(slot):
        return Ff[:, slot * 4096:(slot + 1) * 4096]

    SX = Aa[:, 0:16384]
    KX = Bb[:, 0:16384]
    kvT = Bb[0:64, 16384:24576]
    kerT = Cc[0:32, 0:8192]
    CTre = Aa[:, 0:8192]
    CTim = Aa[:, 8192:16384]

    blk2 = nc.Block("ph2")
    with blk2:
        @blk2.gpsimd
        def _(g):
            def load_group(gi):
                for si, wd in enumerate([wT_d["q"], wT_d["k"], wT_d["v"]]):
                    src = bass.AP(wd, gi * 128, [[1024, 128], [128 * 1024, 16], [1, 128]])
                    g.dma_start(sW[:, si * 2048:(si + 1) * 2048], src).then_inc(sems["awL"], 16)
                src = bass.AP(woT_d, gi * 128 * D, [[D, 128], [1, 2048]])
                g.dma_start(sW[:, 3 * 2048:4 * 2048], src).then_inc(sems["awL"], 16)
                g.dma_start(svec[:, 0:1], bias_d["q"][gi * 128:(gi + 1) * 128, :]).then_inc(sems["awL"], 16)
                g.dma_start(svec[:, 1:2], bias_d["k"][gi * 128:(gi + 1) * 128, :]).then_inc(sems["awL"], 16)
                g.dma_start(svec[:, 2:3], bias_d["v"][gi * 128:(gi + 1) * 128, :]).then_inc(sems["awL"], 16)
                g.dma_start(svec[:, 3:4], cb_d["q"][gi * 128:(gi + 1) * 128, :]).then_inc(sems["awL"], 16)
                g.dma_start(svec[:, 4:5], cb_d["k"][gi * 128:(gi + 1) * 128, :]).then_inc(sems["awL"], 16)
                g.dma_start(svec[:, 5:6], dsk_d[gi * 128:(gi + 1) * 128, :]).then_inc(sems["awL"], 16)
                g.dma_start(svec[:, 6:9], cw_d["q"][gi * 128:(gi + 1) * 128, :]).then_inc(sems["awL"], 16)
                g.dma_start(svec[:, 9:12], cw_d["k"][gi * 128:(gi + 1) * 128, :]).then_inc(sems["awL"], 16)
                g.dma_start(skerb[:], sker_d[gi * 128:(gi + 1) * 128, :]).then_inc(sems["awL"], 16)

            load_group(0)
            for gi in range(CHG):
                for tt in range(32):
                    g.wait_ge(sems["woE"], 32 * gi + tt + 1)
                    dst = spart_d[tt * 128:(tt + 1) * 128, :]
                    srcb = Hh[:, (tt % 2) * 2048:(tt % 2 + 1) * 2048]
                    wsem = sems["woD0" if tt % 2 == 0 else "woD1"]
                    if gi == 0:
                        g.dma_start(dst, srcb).then_inc(wsem, 16)
                    else:
                        g.dma_start(dst, srcb, accum_op=ALU.add).then_inc(wsem, 16)
                if gi + 1 < CHG:
                    g.wait_ge(sems["woT_"], 128 * (gi + 1))
                    load_group(gi + 1)
            g.wait_ge(sems["woD0"], 16 * 16 * CHG)
            g.wait_ge(sems["woD1"], 16 * 16 * CHG)
            g.collective_compute(
                "ReduceScatter", ALU.add,
                replica_groups=[[0, 1], [2, 3], [4, 5], [6, 7]],
                ins=[spart_d.ap().rearrange("(a p) n -> a (p n)", a=2).opt()],
                outs=[sred_d.ap().opt()],
            ).then_inc(sems["rsC"])

        @blk2.sync
        def _(sy):
            for gi in range(CHG):
                for tc in range(8):
                    it = gi * 8 + tc
                    if tc == 0 and gi >= 1:
                        sy.wait_ge(sems["ptC"], 8 * gi)
                    if it >= 2:
                        sy.wait_ge(sems["pjT"], 3 * (it - 1))
                    buf = Cc if tc % 2 == 0 else Dd
                    src = bass.AP(xnTa_d, tc * 512, [[L, 128], [128 * L, 16], [1, 512]])
                    sy.dma_start(buf[:], src).then_inc(
                        sems["axL0" if tc % 2 == 0 else "axL1"], 16)

        @blk2.tensor
        def _(t):
            for gi in range(CHG):
                t.wait_ge(sems["awL"], 16 * 13 * (gi + 1))
                for tc in range(8):
                    it = gi * 8 + tc
                    t.wait_ge(sems["axL0" if tc % 2 == 0 else "axL1"], 16 * (it // 2 + 1))
                    if it >= 2:
                        t.wait_ge(sems["pjS"], 3 * (it - 1))
                    buf = Cc if tc % 2 == 0 else Dd
                    for si in range(3):
                        bank = (tc % 2) * 3 + si
                        for ht in range(16):
                            mm = t.matmul(pb[bank][:, :],
                                          sW[:, si * 2048 + ht * 128: si * 2048 + (ht + 1) * 128],
                                          buf[:, ht * 512:(ht + 1) * 512],
                                          start=(ht == 0), stop=(ht == 15))
                        mm.then_inc(sems["pjT"], 1)
                t.wait_ge(sems["cvV"], gi + 1)
                kvr = fsl(PK).rearrange("p (a b) -> p b a", b=64)
                for k in range(16):
                    if k >= 2:
                        t.wait_ge(sems["preC"], 16 * gi + k - 1)
                    for jj in range(4):
                        r = 4 * k + jj
                        mm = t.matmul(pb[k % 2][0:64, jj * 128:(jj + 1) * 128],
                                      kvr[:, r, :], scst["I128"][:], start=True, stop=True)
                    mm.then_inc(sems["preP"], 1)
                kerr = skerb[:].rearrange("p (a b) -> p b a", b=64)
                for k in range(16):
                    if k >= 2:
                        t.wait_ge(sems["kpreC"], 16 * gi + k - 1)
                    for jj in range(4):
                        r = 4 * k + jj
                        mm = t.matmul(pb[2 + k % 2][0:32, jj * 128:(jj + 1) * 128],
                                      kerr[:, r, :], scst["I128"][:], start=True, stop=True)
                    mm.then_inc(sems["kpreP"], 1)
                t.wait_ge(sems["preC"], 16 * (gi + 1))
                for k in range(32):
                    if k >= 2:
                        t.wait_ge(sems["s1S"], 32 * gi + k - 1)
                    for s_ in (2 * k, 2 * k + 1):
                        mm = t.matmul(pb[k % 2][:, (s_ % 2) * 256:(s_ % 2 + 1) * 256],
                                      kvT[:, s_ * 128:(s_ + 1) * 128], scst["DFTc"][0:64, :],
                                      start=True, stop=True)
                    mm.then_inc(sems["s1P"], 1)
                t.wait_ge(sems["kpreC"], 16 * (gi + 1))
                for k in range(32):
                    if k >= 2:
                        t.wait_ge(sems["ks1S"], 32 * gi + k - 1)
                    for s_ in (2 * k, 2 * k + 1):
                        mm = t.matmul(pb[2 + k % 2][:, (s_ % 2) * 256:(s_ % 2 + 1) * 256],
                                      kerT[:, s_ * 128:(s_ + 1) * 128], scst["DFTc"][0:32, :],
                                      start=True, stop=True)
                    mm.then_inc(sems["ks1P"], 1)
                t.wait_ge(sems["ks1V"], 32 * (gi + 1))
                for k in range(32):
                    if k >= 2:
                        t.wait_ge(sems["k2S"], 32 * gi + k - 1)
                    for s_ in (2 * k, 2 * k + 1):
                        t.matmul(pb[k % 2][:, (s_ % 2) * 256:(s_ % 2 + 1) * 256],
                                 scst["B_re"][:], KX[:, s_ * 256:(s_ + 1) * 256],
                                 start=True, stop=True)
                    for s_ in (2 * k, 2 * k + 1):
                        mm = t.matmul(pb[2 + k % 2][:, (s_ % 2) * 256:(s_ % 2 + 1) * 256],
                                      scst["B_im"][:], KX[:, s_ * 256:(s_ + 1) * 256],
                                      start=True, stop=True)
                    mm.then_inc(sems["k2P"], 1)
                t.wait_ge(sems["s1V"], 32 * (gi + 1))
                t.wait_ge(sems["k2V"], 32 * (gi + 1))
                for k in range(32):
                    if k >= 2:
                        t.wait_ge(sems["s2S"], 32 * gi + k - 1)
                    for s_ in (2 * k, 2 * k + 1):
                        t.matmul(pb[k % 2][:, (s_ % 2) * 256:(s_ % 2 + 1) * 256],
                                 scst["B_re"][:], SX[:, s_ * 256:(s_ + 1) * 256],
                                 start=True, stop=True)
                    for s_ in (2 * k, 2 * k + 1):
                        mm = t.matmul(pb[2 + k % 2][:, (s_ % 2) * 256:(s_ % 2 + 1) * 256],
                                      scst["B_im"][:], SX[:, s_ * 256:(s_ + 1) * 256],
                                      start=True, stop=True)
                    mm.then_inc(sems["s2P"], 1)
                t.wait_ge(sems["s2V"], 32 * (gi + 1))
                for k in range(32):
                    if k >= 2:
                        t.wait_ge(sems["i1S"], 32 * gi + k - 1)
                    for s_ in (2 * k, 2 * k + 1):
                        t.matmul(pb[k % 2][:, (s_ % 2) * 256:(s_ % 2 + 1) * 256],
                                 scst["C_re"][:], SX[:, s_ * 256:(s_ + 1) * 256],
                                 start=True, stop=True)
                    for s_ in (2 * k, 2 * k + 1):
                        mm = t.matmul(pb[2 + k % 2][:, (s_ % 2) * 256:(s_ % 2 + 1) * 256],
                                      scst["C_im"][:], SX[:, s_ * 256:(s_ + 1) * 256],
                                      start=True, stop=True)
                    mm.then_inc(sems["i1P"], 1)
                t.wait_ge(sems["i1V"], 32 * (gi + 1))
                for k in range(32):
                    if k >= 2:
                        t.wait_ge(sems["mtS"], 32 * gi + k - 1)
                    for jj, s_ in enumerate((2 * k, 2 * k + 1)):
                        t.matmul(pb[k % 2][:, jj * 128:(jj + 1) * 128],
                                 KX[:, s_ * 256:s_ * 256 + 128], scst["I128"][:],
                                 start=True, stop=True)
                        mm = t.matmul(pb[2 + k % 2][:, jj * 128:(jj + 1) * 128],
                                      KX[:, s_ * 256 + 128:s_ * 256 + 256], scst["I128"][:],
                                      start=True, stop=True)
                    mm.then_inc(sems["mtP"], 1)
                t.wait_ge(sems["mtS"], 32 * (gi + 1))
                for c_ in range(16):
                    if c_ >= 2:
                        t.wait_ge(sems["i2S"], 16 * gi + c_ - 1)
                    t.matmul(pb[4 + c_ % 2][:, :], scst["ID_re"][:],
                             CTre[:, c_ * 512:(c_ + 1) * 512], start=True, stop=False)
                    mm = t.matmul(pb[4 + c_ % 2][:, :], scst["ID_imN"][:],
                                  CTim[:, c_ * 512:(c_ + 1) * 512], start=False, stop=True)
                    mm.then_inc(sems["i2P"], 1)
                t.wait_ge(sems["i2S"], 16 * (gi + 1))
                Y2r = Y2b[0:64, :].rearrange("p (c r) -> p r c", r=64)
                for k in range(8):
                    if k >= 2:
                        t.wait_ge(sems["ptC"], 8 * gi + k - 1)
                    for jj in range(8):
                        n1 = 8 * k + jj
                        mm = t.matmul(pb[6 + k % 2][:, jj * 64:(jj + 1) * 64],
                                      Y2r[:, n1, :], scst["I64"][:], start=True, stop=True)
                    mm.then_inc(sems["ptP"], 1)
                t.wait_ge(sems["yV"], gi + 1)
                yp = fsl(PQ)
                for tt in range(32):
                    for hc in range(4):
                        ii_ = tt * 4 + hc
                        if ii_ >= 2:
                            t.wait_ge(sems["woS"], 128 * gi + ii_ - 1)
                        mm = t.matmul(pb[4 + ii_ % 2][:, :],
                                      yp[:, tt * 128:(tt + 1) * 128],
                                      sW[:, 3 * 2048 + hc * 512: 3 * 2048 + (hc + 1) * 512],
                                      start=True, stop=(gi != 0))
                        if gi == 0:
                            mm = t.matmul(pb[4 + ii_ % 2][:, :], scst["ONESR"][:],
                                          sbo[0:1, hc * 512:(hc + 1) * 512],
                                          start=False, stop=True)
                        mm.then_inc(sems["woT_"], 1)

        @blk2.scalar
        def _(s):
            for gi in range(CHG):
                for tc in range(8):
                    it = gi * 8 + tc
                    for si in range(3):
                        s.wait_ge(sems["pjT"], 3 * it + si + 1)
                        bank = (tc % 2) * 3 + si
                        slot = (PQ, PK, PV)[si]
                        s.copy(fsl(slot)[:, tc * 512:(tc + 1) * 512],
                               pb[bank][:, :]).then_inc(sems["pjS"], 1)
                for k in range(32):
                    s.wait_ge(sems["s1P"], 32 * gi + k + 1)
                    if k >= 2:
                        s.wait_ge(sems["s1V"], 32 * gi + k - 1)
                    s.copy(ebuf[:, (k % 2) * 512:(k % 2 + 1) * 512],
                           pb[k % 2][:, :]).then_inc(sems["s1S"], 1)
                s.wait_ge(sems["s1V"], 32 * (gi + 1))
                for k in range(32):
                    s.wait_ge(sems["ks1P"], 32 * gi + k + 1)
                    if k >= 2:
                        s.wait_ge(sems["ks1V"], 32 * gi + k - 1)
                    s.copy(ebuf[:, (k % 2) * 512:(k % 2 + 1) * 512],
                           pb[2 + k % 2][:, :]).then_inc(sems["ks1S"], 1)
                for k in range(32):
                    s.wait_ge(sems["k2P"], 32 * gi + k + 1)
                    if k >= 2:
                        s.wait_ge(sems["k2V"], 32 * gi + k - 1)
                    base = (k % 2) * 1024
                    s.copy(e2buf[:, base:base + 512], pb[k % 2][:, :])
                    s.copy(e2buf[:, base + 512:base + 1024],
                           pb[2 + k % 2][:, :]).then_inc(sems["k2S"], 1)
                for k in range(32):
                    s.wait_ge(sems["s2P"], 32 * gi + k + 1)
                    if k >= 2:
                        s.wait_ge(sems["s2V"], 32 * gi + k - 1)
                    base = (k % 2) * 1024
                    s.copy(e2buf[:, base:base + 512], pb[k % 2][:, :])
                    s.copy(e2buf[:, base + 512:base + 1024],
                           pb[2 + k % 2][:, :]).then_inc(sems["s2S"], 1)
                for k in range(32):
                    s.wait_ge(sems["i1P"], 32 * gi + k + 1)
                    if k >= 2:
                        s.wait_ge(sems["i1V"], 32 * gi + k - 1)
                    base = (k % 2) * 1024
                    s.copy(e2buf[:, base:base + 512], pb[k % 2][:, :])
                    s.copy(e2buf[:, base + 512:base + 1024],
                           pb[2 + k % 2][:, :]).then_inc(sems["i1S"], 1)
                for k in range(32):
                    s.wait_ge(sems["mtP"], 32 * gi + k + 1)
                    s.copy(CTre[:, k * 256:(k + 1) * 256], pb[k % 2][:, 0:256])
                    s.copy(CTim[:, k * 256:(k + 1) * 256],
                           pb[2 + k % 2][:, 0:256]).then_inc(sems["mtS"], 1)
                for c_ in range(16):
                    s.wait_ge(sems["i2P"], 16 * gi + c_ + 1)
                    s.copy(Y2b[:, c_ * 512:(c_ + 1) * 512],
                           pb[4 + c_ % 2][:, :]).then_inc(sems["i2S"], 1)
                for tt in range(32):
                    for hc in range(4):
                        ii_ = tt * 4 + hc
                        s.wait_ge(sems["woT_"], 128 * gi + ii_ + 1)
                        if hc == 0 and tt >= 2:
                            s.wait_ge(sems["woD0" if tt % 2 == 0 else "woD1"],
                                      16 * (16 * gi + (tt - 2) // 2 + 1))
                        s.copy(Hh[:, (tt % 2) * 2048 + hc * 512:(tt % 2) * 2048 + (hc + 1) * 512],
                               pb[4 + ii_ % 2][:, :]).then_inc(sems["woS"], 1)
                    s.drain().then_inc(sems["woE"], 1)

        @blk2.vector
        def _(v):
            for gi in range(CHG):
                v.wait_ge(sems["pjS"], 24 * (gi + 1))
                q_, k_, v_ = fsl(PQ), fsl(PK), fsl(PV)
                qc, kc = fsl(PQC), fsl(PKC)
                v.tensor_tensor(svec[:, 22:23], svec[:, 7:8], svec[:, 8:9], ALU.add)
                v.tensor_tensor(svec[:, 24:25], svec[:, 10:11], svec[:, 11:12], ALU.add)
                v.tensor_tensor(svec[:, 23:24], svec[:, 22:23], svec[:, 6:7], ALU.add)
                v.tensor_tensor(svec[:, 25:26], svec[:, 24:25], svec[:, 9:10], ALU.add)
                for (w0c, bcol, cbcol, o, t2) in ((6, 0, 3, 16, 22), (9, 1, 4, 19, 24)):
                    v.scalar_tensor_tensor(svec[:, o:o + 1], svec[:, bcol:bcol + 1],
                                           svec[:, t2 + 1:t2 + 2], svec[:, cbcol:cbcol + 1],
                                           ALU.mult, ALU.add)
                    v.scalar_tensor_tensor(svec[:, o + 1:o + 2], svec[:, bcol:bcol + 1],
                                           svec[:, w0c + 2:w0c + 3], svec[:, cbcol:cbcol + 1],
                                           ALU.mult, ALU.add)
                    v.scalar_tensor_tensor(svec[:, o + 2:o + 3], svec[:, bcol:bcol + 1],
                                           svec[:, t2:t2 + 1], svec[:, cbcol:cbcol + 1],
                                           ALU.mult, ALU.add)
                for (sig, dst, w0c, o) in ((q_, qc, 6, 16), (k_, kc, 9, 19)):
                    v.tensor_scalar(dst, sig, svec[:, w0c + 2:w0c + 3], None, ALU.mult)
                    v.scalar_tensor_tensor(dst[:, 1:4096], sig[:, 0:4095],
                                           svec[:, w0c + 1:w0c + 2], dst[:, 1:4096],
                                           ALU.mult, ALU.add)
                    v.scalar_tensor_tensor(dst[:, 2:4096], sig[:, 0:4094],
                                           svec[:, w0c:w0c + 1], dst[:, 2:4096],
                                           ALU.mult, ALU.add)
                    v.tensor_scalar(dst[:, 2:4096], dst[:, 2:4096], svec[:, o:o + 1], None, ALU.add)
                    v.tensor_scalar(dst[:, 0:1], dst[:, 0:1], svec[:, o + 1:o + 2], None, ALU.add)
                    v.tensor_scalar(dst[:, 1:2], dst[:, 1:2], svec[:, o + 2:o + 3], None, ALU.add)
                v.tensor_scalar(v_, v_, svec[:, 2:3], None, ALU.add)
                v.tensor_tensor(k_, kc, v_, ALU.mult)
                v.tensor_scalar(v_, k_, svec[:, 5:6], None, ALU.mult).then_inc(sems["cvV"], 1)

                kvTw = kvT.rearrange("p (c r) -> p r c", r=64)
                for k in range(16):
                    v.wait_ge(sems["preP"], 16 * gi + k + 1)
                    for jj in range(4):
                        r = 4 * k + jj
                        cp = v.tensor_copy(kvTw[:, r, :], pb[k % 2][0:64, jj * 128:(jj + 1) * 128])
                    cp.then_inc(sems["preC"], 1)
                kerTw = kerT.rearrange("p (c r) -> p r c", r=64)
                for k in range(16):
                    v.wait_ge(sems["kpreP"], 16 * gi + k + 1)
                    for jj in range(4):
                        r = 4 * k + jj
                        cp = v.tensor_copy(kerTw[:, r, :], pb[2 + k % 2][0:32, jj * 128:(jj + 1) * 128])
                    cp.then_inc(sems["kpreC"], 1)

                def r3(ap, x=256):
                    return ap.rearrange("p (s x) -> p s x", x=x)

                def tw(dstbuf, s0, TRE, TIM, sem):
                    sr = r3(ebuf[:, ((s0 // 2) % 2) * 512:((s0 // 2) % 2 + 1) * 512])
                    a_re, a_im = sr[:, :, 0:128], sr[:, :, 128:256]
                    dre = r3(dstbuf)[:, s0:s0 + 2, 0:128]
                    dim = r3(dstbuf)[:, s0:s0 + 2, 128:256]
                    tre = r3(scst[TRE][:])[:, :, 0:128]
                    tim = r3(scst[TIM][:])[:, :, 0:128]
                    ta = r3(tmpA)[:, :, 0:128]
                    v.tensor_tensor(dre, a_re, tre, ALU.mult)
                    v.tensor_tensor(ta, a_im, tim, ALU.mult)
                    v.tensor_tensor(dre, dre, ta, ALU.subtract)
                    v.tensor_tensor(dim, a_re, tim, ALU.mult)
                    v.tensor_tensor(ta, a_im, tre, ALU.mult)
                    v.tensor_tensor(dim, dim, ta, ALU.add).then_inc(sem, 1)

                for k in range(32):
                    v.wait_ge(sems["s1S"], 32 * gi + k + 1)
                    tw(SX, 2 * k, "T_reB", "T_imB", sems["s1V"])
                for k in range(32):
                    v.wait_ge(sems["ks1S"], 32 * gi + k + 1)
                    tw(KX, 2 * k, "T_reB", "T_imB", sems["ks1V"])
                for k in range(32):
                    v.wait_ge(sems["k2S"], 32 * gi + k + 1)
                    base = (k % 2) * 1024
                    p1 = r3(e2buf[:, base:base + 512])
                    p2 = r3(e2buf[:, base + 512:base + 1024])
                    kre = r3(KX)[:, 2 * k:2 * k + 2, 0:128]
                    kim = r3(KX)[:, 2 * k:2 * k + 2, 128:256]
                    v.tensor_tensor(kre, p1[:, :, 0:128], p2[:, :, 128:256], ALU.subtract)
                    v.tensor_tensor(kim, p1[:, :, 128:256], p2[:, :, 0:128],
                                    ALU.add).then_inc(sems["k2V"], 1)
                for k in range(32):
                    v.wait_ge(sems["s2S"], 32 * gi + k + 1)
                    base = (k % 2) * 1024
                    p1 = r3(e2buf[:, base:base + 512])
                    p2 = r3(e2buf[:, base + 512:base + 1024])
                    u = r3(tmpU)[:, :, 0:128]
                    w = r3(tmpV)[:, :, 0:128]
                    ta = r3(tmpA)[:, :, 0:128]
                    v.tensor_tensor(u, p1[:, :, 0:128], p2[:, :, 128:256], ALU.subtract)
                    v.tensor_tensor(w, p1[:, :, 128:256], p2[:, :, 0:128], ALU.add)
                    kx = r3(KX)[:, 2 * k:2 * k + 2, :]
                    kre, kim = kx[:, :, 0:128], kx[:, :, 128:256]
                    ppr = r3(SX)[:, 2 * k:2 * k + 2, 0:128]
                    ppi = r3(SX)[:, 2 * k:2 * k + 2, 128:256]
                    v.tensor_tensor(ppr, u, kre, ALU.mult)
                    v.tensor_tensor(ta, w, kim, ALU.mult)
                    v.tensor_tensor(ppr, ppr, ta, ALU.subtract)
                    v.tensor_tensor(ppi, u, kim, ALU.mult)
                    v.tensor_tensor(ta, w, kre, ALU.mult)
                    v.tensor_tensor(ppi, ppi, ta, ALU.add).then_inc(sems["s2V"], 1)
                for k in range(32):
                    v.wait_ge(sems["i1S"], 32 * gi + k + 1)
                    base = (k % 2) * 1024
                    p1 = r3(e2buf[:, base:base + 512])
                    p2 = r3(e2buf[:, base + 512:base + 1024])
                    u = r3(tmpU)[:, :, 0:128]
                    w = r3(tmpV)[:, :, 0:128]
                    ta = r3(tmpA)[:, :, 0:128]
                    v.tensor_tensor(u, p1[:, :, 0:128], p2[:, :, 128:256], ALU.subtract)
                    v.tensor_tensor(w, p1[:, :, 128:256], p2[:, :, 0:128], ALU.add)
                    tre = r3(scst["Ti_reB"][:])[:, :, 0:128]
                    tim = r3(scst["Ti_imB"][:])[:, :, 0:128]
                    t1r = r3(KX)[:, 2 * k:2 * k + 2, 0:128]
                    t1i = r3(KX)[:, 2 * k:2 * k + 2, 128:256]
                    v.tensor_tensor(t1r, u, tre, ALU.mult)
                    v.tensor_tensor(ta, w, tim, ALU.mult)
                    v.tensor_tensor(t1r, t1r, ta, ALU.subtract)
                    v.tensor_tensor(t1i, u, tim, ALU.mult)
                    v.tensor_tensor(ta, w, tre, ALU.mult)
                    v.tensor_tensor(t1i, t1i, ta, ALU.add).then_inc(sems["i1V"], 1)
                yw = fsl(PKC).rearrange("p (a b) -> p b a", b=64)
                for k in range(8):
                    v.wait_ge(sems["ptP"], 8 * gi + k + 1)
                    for jj in range(8):
                        n1 = 8 * k + jj
                        cp = v.tensor_copy(yw[:, n1, :], pb[6 + k % 2][:, jj * 64:(jj + 1) * 64])
                    cp.then_inc(sems["ptC"], 1)
                v.tensor_tensor(fsl(PKC), fsl(PKC), fsl(PV), ALU.add)
                v.tensor_tensor(fsl(PQ), fsl(PKC), fsl(PQC), ALU.mult).then_inc(sems["yV"], 1)


    if PHASE_LIMIT < 3:
        _emit_taps()
        ctx.close()
        return nc
    blk3 = nc.Block("ph3")

    def gu_src(e_, it):
        if e_ < 16:
            base = (e_ // 2) * RBLK1 + (e_ % 2) * GU
            return bass.AP(ccout1_d, base + it * 128, [[3072, 128], [128 * 3072, 16], [1, 128]])
        r_ = it // 3
        base = r_ * RBLK2 + OFF_SGU
        return bass.AP(ccout2_d, base + (it % 3) * 128, [[384, 128], [128 * 384, 16], [1, 128]])

    mdw_tot = []
    acc = 0
    for tq in range(4):
        for e_ in range(NE):
            acc += 96 if e_ == 16 else 12
            mdw_tot.append(acc)

    def mdw_after(eidx):
        return mdw_tot[eidx]

    with blk3:
        @blk3.sync
        def _(sy):
            def finale_io(tq):
                for tl in range(4):
                    n_ = 4 * tq + tl
                    row = tq * 512 + tl * 128
                    if n_ == 0:
                        sy.wait_ge(sems["rsC"], 1)
                    if n_ >= 1:
                        sy.wait_ge(sems["fiV"], n_)
                    src = bass.AP(sred_d, row * D, [[D, 128], [1, D]])
                    sy.dma_start(stg[:, 0:2048], src).then_inc(sems["fiL"], 16)
                    sy.dma_start(xfin, xo_d[row:row + 128, :]).then_inc(sems["fiL"], 16)
                    sy.wait_ge(sems["fiV"], n_ + 1)
                    sy.dma_start(out_d[row:row + 128, :], xfin).then_inc(sems["outD"], 16)

            for tq in range(4):
                if tq >= 1:
                    sy.wait_ge(sems["mguT"], (24 * NE) * tq)
                for ht in range(16):
                    src = bass.AP(xnTo_d, ht * 128 * TOK + tq * 512, [[TOK, 128], [1, 512]])
                    sy.dma_start(Cc[:, ht * 512:(ht + 1) * 512], src).then_inc(sems["mxL"], 16)
                for e_ in range(NE):
                    eidx = tq * NE + e_
                    for it in range(24):
                        git = eidx * 24 + it
                        if git >= 2:
                            sy.wait_ge(sems["mguT"], git - 1)
                        sy.dma_start(Dd[:, (it % 2) * 2048:(it % 2 + 1) * 2048],
                                     gu_src(e_, it)).then_inc(
                            sems["mgwL0" if it % 2 == 0 else "mgwL1"], 16)
                    if eidx >= 1:
                        sy.wait_ge(sems["mdT"], 16 * eidx)
                    if e_ < 16:
                        base = (e_ // 2) * RBLK2 + OFF_DN + (e_ % 2) * DN
                        for it in range(12):
                            src = bass.AP(ccout2_d, base + it * 128 * 2048, [[2048, 128], [1, 2048]])
                            sy.dma_start(Bb[:, it * 2048:(it + 1) * 2048], src).then_inc(sems["mdwL"], 16)
                    else:
                        for it in range(12):
                            for r_ in range(8):
                                src = bass.AP(ccout2_d, r_ * RBLK2 + OFF_SDN + it * 128 * 256,
                                              [[256, 128], [1, 256]])
                                sy.dma_start(Bb[:, it * 2048 + r_ * 256: it * 2048 + (r_ + 1) * 256],
                                             src).then_inc(sems["mdwL"], 16)
                finale_io(tq)
            sy.wait_ge(sems["outD"], 16 * 16)

        @blk3.tensor
        def _(t):
            t.wait_ge(sems["agC"], 2)
            for tq in range(4):
                t.wait_ge(sems["mxL"], 16 * 16 * (tq + 1))
                for tl in range(4):
                    n_ = 4 * tq + tl
                    if n_ >= 1:
                        t.wait_ge(sems["gV"], 3 * (n_ - 1) + 1)
                    for ht in range(16):
                        mm = t.matmul(pb[7][:, 0:16],
                                      Cc[:, ht * 512 + tl * 128: ht * 512 + (tl + 1) * 128],
                                      sgwT[:, ht * 16:(ht + 1) * 16],
                                      start=(ht == 0), stop=(ht == 15))
                    mm.then_inc(sems["gT"], 1)
                for e_ in range(NE):
                    eidx = tq * NE + e_
                    for it in range(24):
                        git = eidx * 24 + it
                        t.wait_ge(sems["mgwL0" if it % 2 == 0 else "mgwL1"],
                                  16 * (eidx * 12 + it // 2 + 1))
                        if git >= 2:
                            t.wait_ge(sems["mguS"], git - 1)
                        for ht in range(16):
                            mm = t.matmul(pb[it % 2][:, :],
                                          Dd[:, (it % 2) * 2048 + ht * 128:(it % 2) * 2048 + (ht + 1) * 128],
                                          Cc[:, ht * 512:(ht + 1) * 512],
                                          start=(ht == 0), stop=(ht == 15))
                        mm.then_inc(sems["mguT"], 1)
                    t.wait_ge(sems["mdwL"], 16 * mdw_after(eidx))
                    t.wait_ge(sems["mguV"], 12 * (eidx + 1))
                    for tt in range(4):
                        for hc in range(4):
                            di = tt * 4 + hc
                            gdi = eidx * 16 + di
                            if gdi >= 2:
                                t.wait_ge(sems["mdV"], gdi - 1)
                            for it in range(12):
                                mm = t.matmul(pb[2 + di % 2][:, :],
                                              Aa[:, it * 512 + tt * 128: it * 512 + tt * 128 + 128],
                                              Bb[:, it * 2048 + hc * 512: it * 2048 + (hc + 1) * 512],
                                              start=(it == 0), stop=(it == 11))
                            mm.then_inc(sems["mdT"], 1)

        @blk3.scalar
        def _(s):
            for _w in range(4):
                s.activation(Hh[:, 2048:2064], Hh[:, 2064:2080], AF.Exp,
                             bias=svec[:, 14:15], scale=1.0)
            s.drain()
            for tq in range(4):
                for tl in range(4):
                    n_ = 4 * tq + tl
                    s.wait_ge(sems["gV"], 3 * n_ + 2)
                    s.activation(lgb[:, 96:112], lgb[:, 32:48], AF.Exp,
                                 bias=redb[:, 8:9], scale=1.0).then_inc(sems["gS"], 1)
                for e_ in range(NE):
                    eidx = tq * NE + e_
                    if eidx >= 1:
                        s.wait_ge(sems["mdT"], 16 * eidx)
                    for it in range(12):
                        git = eidx * 24 + it
                        s.wait_ge(sems["mguT"], git + 1)
                        s.activation(Aa[:, it * 512:(it + 1) * 512], pb[it % 2][:, :],
                                     AF.Silu).then_inc(sems["mguS"], 1)
                    for it in range(12, 24):
                        git = eidx * 24 + it
                        s.wait_ge(sems["mguT"], git + 1)
                        s.copy(Aa[:, it * 512:(it + 1) * 512],
                               pb[it % 2][:, :]).then_inc(sems["mguS"], 1)
                for tl in range(4):
                    n_ = 4 * tq + tl
                    s.wait_ge(sems["mdV"], 16 * NE * (tq + 1))
                    if n_ >= 1:
                        s.wait_ge(sems["fiV"], n_)
                    s.activation(Hh[:, 0:2048], accA[:, tl * 2048:(tl + 1) * 2048],
                                 AF.Silu).then_inc(sems["gSi"], 1)

        @blk3.vector
        def _(v):
            def VJ():
                v.tensor_tensor(redb[:, 15:16], redb[:, 15:16], redb[:, 15:16], ALU.max)

            for tq in range(4):
                for tl in range(4):
                    n_ = 4 * tq + tl
                    v.wait_ge(sems["gT"], n_ + 1)
                    lg = lgb[:, 0:16]
                    v.tensor_copy(lg, pb[7][:, 0:16])
                    v.tensor_copy(lgb[:, 32:48], pb[7][:, 0:16])
                    v.memset(lgb[:, 16:32], 0.0).then_inc(sems["gV"], 1)
                    for r_ in range(4):
                        if r_ >= 1:
                            VJ()
                        v.reduce_max(redb[:, 5:6], lg, axis=AX.X)
                        VJ()
                        if r_ == 0:
                            v.tensor_copy(redb[:, 6:7], redb[:, 5:6])
                        v.tensor_scalar(lgb[:, 48:64], lg, redb[:, 5:6], None, ALU.is_equal)
                        VJ()
                        v.tensor_tensor(lgb[:, 64:80], lgb[:, 48:64], colv[:, 0:16], ALU.mult)
                        VJ()
                        v.reduce_max(redb[:, 7:8], lgb[:, 64:80], axis=AX.X)
                        VJ()
                        v.tensor_scalar(lgb[:, 80:96], lgb[:, 64:80], redb[:, 7:8], None, ALU.is_equal)
                        VJ()
                        v.tensor_tensor(lgb[:, 16:32], lgb[:, 16:32], lgb[:, 80:96], ALU.max)
                        v.scalar_tensor_tensor(lg, lgb[:, 80:96], -1e30, lg, ALU.mult, ALU.add)
                    v.tensor_scalar(redb[:, 8:9], redb[:, 6:7], -1.0, None, ALU.mult).then_inc(sems["gV"], 1)
                    v.wait_ge(sems["gS"], n_ + 1)
                    v.tensor_tensor(lgb[:, 96:112], lgb[:, 96:112], lgb[:, 16:32], ALU.mult)
                    VJ()
                    v.reduce_sum(redb[:, 9:10], lgb[:, 96:112], axis=AX.X)
                    VJ()
                    v.reciprocal(redb[:, 10:11], redb[:, 9:10])
                    VJ()
                    v.tensor_scalar(comb[:, tl * 16:(tl + 1) * 16], lgb[:, 96:112],
                                    redb[:, 10:11], None, ALU.mult).then_inc(sems["gV"], 1)
                for e_ in range(NE):
                    eidx = tq * NE + e_
                    for it in range(12):
                        v.wait_ge(sems["mguS"], eidx * 24 + 12 + it + 1)
                        cp = v.tensor_tensor(Aa[:, it * 512:(it + 1) * 512],
                                             Aa[:, it * 512:(it + 1) * 512],
                                             Aa[:, (12 + it) * 512:(13 + it) * 512], ALU.mult)
                        cp.then_inc(sems["mguV"], 1)
                    for tt in range(4):
                        for hc in range(4):
                            di = tt * 4 + hc
                            gdi = eidx * 16 + di
                            v.wait_ge(sems["mdT"], gdi + 1)
                            acc_ = accA[:, tt * 2048 + hc * 512: tt * 2048 + (hc + 1) * 512]
                            if e_ == 0:
                                cp = v.tensor_scalar(acc_, pb[2 + di % 2][:, :],
                                                     comb[:, tt * 16 + e_:tt * 16 + e_ + 1],
                                                     None, ALU.mult)
                            elif e_ < 16:
                                cp = v.scalar_tensor_tensor(acc_, pb[2 + di % 2][:, :],
                                                            comb[:, tt * 16 + e_:tt * 16 + e_ + 1],
                                                            acc_, ALU.mult, ALU.add)
                            else:
                                cp = v.tensor_tensor(acc_, acc_, pb[2 + di % 2][:, :], ALU.add)
                            cp.then_inc(sems["mdV"], 1)
                v.wait_ge(sems["rsC"], 1)
                for tl in range(4):
                    n_ = 4 * tq + tl
                    v.wait_ge(sems["fiL"], 32 * (n_ + 1))
                    v.wait_ge(sems["gSi"], n_ + 1)
                    v.tensor_tensor(stg[:, 2048:4096], stg[:, 0:2048], Hh[:, 0:2048], ALU.mult)
                    v.tensor_tensor(xfin, xfin,
                                    stg[:, 2048:4096], ALU.add).then_inc(sems["fiV"], 1)


    _emit_taps()
    ctx.close()
    return nc


_NC_CACHE = None


def _get_nc():
    global _NC_CACHE
    if _NC_CACHE is None:
        _NC_CACHE = _build_nc()
    return _NC_CACHE


def kernel(**inputs):
    cst = _consts()
    f32 = np.float32

    def C(x):
        return np.ascontiguousarray(x, f32)

    u = np.asarray(inputs["L0_attn_u"])
    Wq, Wk, Wv = (np.asarray(inputs["L0_attn_W" + s]) for s in "qkv")
    bq, bk, bv = (np.asarray(inputs["L0_attn_b" + s]) for s in "qkv")
    cqw = np.asarray(inputs["L0_attn_convq_w"])
    cqb = np.asarray(inputs["L0_attn_convq_b"])
    ckw = np.asarray(inputs["L0_attn_convk_w"])
    ckb = np.asarray(inputs["L0_attn_convk_b"])
    sker = np.asarray(inputs["L0_attn_ssm_kernel"])
    dsk = np.asarray(inputs["L0_attn_D_skip"])
    Wo = np.asarray(inputs["L0_attn_Wo"])
    bo = np.asarray(inputs["L0_attn_bo"])
    gw = np.asarray(inputs["L0_ffn_gate_w"])
    gupk = np.ascontiguousarray(inputs["L0_ffn_gate_up_packed"], np.int32)
    gusc = np.asarray(inputs["L0_ffn_gate_up_scales"])
    dnpk = np.ascontiguousarray(inputs["L0_ffn_down_packed"], np.int32)
    dnsc = np.asarray(inputs["L0_ffn_down_scales"])
    sgupk = np.ascontiguousarray(inputs["L0_ffn_shared_gate_up_packed"], np.int32)
    sgusc = np.asarray(inputs["L0_ffn_shared_gate_up_scales"])
    sdnpk = np.ascontiguousarray(inputs["L0_ffn_shared_down_packed"], np.int32)
    sdnsc = np.asarray(inputs["L0_ffn_shared_down_scales"])

    colvf = np.tile((16.0 - np.arange(16, dtype=f32))[None, :], (128, 1))

    in_maps = []
    for c in range(8):
        b_, j_ = c >> 1, c & 1
        sl = slice(1024 * j_, 1024 * j_ + 1024)
        m = dict(
            xb=C(u[b_]),
            xo=C(u[b_, 2048 * j_:2048 * j_ + 2048]),
            wqT=C(Wq[sl].T), wkT=C(Wk[sl].T), wvT=C(Wv[sl].T),
            bq=C(bq[sl][:, None]), bk=C(bk[sl][:, None]), bv=C(bv[sl][:, None]),
            cqw=C(cqw[sl, 0, :]), ckw=C(ckw[sl, 0, :]),
            cqb=C(cqb[sl][:, None]), ckb=C(ckb[sl][:, None]),
            sker=C(sker[sl]),
            dsk=C(dsk[sl][:, None]),
            woT=C(Wo[:, sl].T),
            bo=C(bo[None, :]),
            bomask=C(np.array([[1.0 if j_ == 0 else 0.0]])),
            gwT=C(gw.T),
            colvf=C(colvf),
            gupk=np.ascontiguousarray(gupk[2 * c:2 * c + 2]),
            gusc=C(gusc[2 * c:2 * c + 2]),
            dnpk=np.ascontiguousarray(dnpk[2 * c:2 * c + 2]),
            dnsc=C(dnsc[2 * c:2 * c + 2]),
            sgupk=np.ascontiguousarray(sgupk[:, 384 * c:384 * c + 384]),
            sgusc=C(sgusc[:, 384 * c:384 * c + 384]),
            sdnpk=np.ascontiguousarray(sdnpk[:, 256 * c:256 * c + 256]),
            sdnsc=C(sdnsc[:, 256 * c:256 * c + 256]),
        )
        m.update(cst)
        in_maps.append(m)

    nc = _get_nc()
    res = run_bass_kernel_spmd(nc, in_maps, core_ids=list(range(8)))
    out = np.zeros((4, 4096, 2048), np.float32)
    for c in range(8):
        b_, j_ = c >> 1, c & 1
        out[b_, 2048 * j_:2048 * j_ + 2048] = res.results[c]["out"]
    return out
